# version 15
# speedup vs baseline: 1.2994x; 1.0074x over previous
import numpy as np
import ml_dtypes

import concourse.bass as bass
import concourse.mybir as mybir
from concourse.bass_isa import ReduceOp
from concourse.bass_utils import run_bass_kernel_spmd

F32 = mybir.dt.float32
BF16 = mybir.dt.bfloat16
I32 = mybir.dt.int32
AF = mybir.ActivationFunctionType
ALU = mybir.AluOpType
AX = mybir.AxisListType

T, D = 2048, 1024
NT, DC = 16, 8
DEBUG = False
PHASE = "full"
INNER = 4096
EPS = 1e-5


class Wk:

    def __init__(self, cur, eng, sem):
        self.cur = cur
        self.eng = eng
        self.sem = sem
        self.cnt = {}

    def v(self, s):
        return self.cnt.get(s, 0)

    def bump(self, s, n):
        self.cnt[s] = self.cnt.get(s, 0) + n
        return self.cnt[s]

    def wait(self, engine, s, val):
        if val is None or val <= 0:
            return
        if self.cur == engine:
            self.eng.wait_ge(self.sem[s], val)

    def do(self, engine, fn, inc=None, n=1):
        if self.cur == engine:
            ins = fn(self.eng)
            if inc is not None:
                ins.then_inc(self.sem[inc], n)
        if inc is not None:
            return self.bump(inc, n)
        return None

    def dma(self, engine, out, in_, inc, transpose=False):
        if transpose:
            return self.do(engine, lambda e: e.dma_start_transpose(out, in_),
                           inc=inc, n=16)
        return self.do(engine, lambda e: e.dma_start(out=out, in_=in_),
                       inc=inc, n=16)


def build_nc():
    nc = bass.Bass("TRN2", target_bir_lowering=False,
                   detect_race_conditions=False)

    def din(name, shape, dtype=F32):
        return nc.dram_tensor(name, shape, dtype, kind="ExternalInput")

    x_in = din("x_in", [T, D])
    qkvw = [din("qkvw0", [D, 3 * D]), din("qkvw1", [D, 3 * D])]
    qkvb0 = din("qkvb0", [128, 24])
    qvb0 = din("qvb0", [1, D])
    oww = [din("ow0", [D, D]), din("ow1", [D, D])]
    obb = [din("ob0", [1, D]), din("ob1", [1, D])]
    relpT = din("relpT", [64, 4096])
    rpew = din("rpew", [64, 1])
    Wexp = din("Wexp", [8, D, D])
    bexp = din("bexp", [8, D])
    logit8 = din("logit8", [1, 8])
    u8 = din("u8", [1, 8])
    wgT = din("wgT", [D, INNER])
    wuT = din("wuT", [D, INNER])
    wdT = din("wdT", [INNER, D])
    cosTc = din("cosTc", [128, T])
    sinTc = din("sinTc", [128, T])
    tri_c = din("tri_c", [128, 128])
    p128c = din("p128c", [128, 128])
    iota8 = din("iota8", [1, 8])

    y_out = nc.dram_tensor("y_out", [T, D], F32, kind="ExternalOutput")

    kind_i = "ExternalOutput" if DEBUG else "Internal"
    ebv_h = nc.dram_tensor("ebv_h", [1, 4608], BF16, kind="Internal")
    xh_h = nc.dram_tensor("xh_h", [T, D], BF16, kind=kind_i)
    h_h = nc.dram_tensor("h_h", [T, INNER], F32, kind=kind_i)
    hq_h = nc.dram_tensor("hq_h", [T, INNER], BF16, kind=kind_i)
    xp_h = nc.dram_tensor("xp_h", [T, D], F32, kind=kind_i)
    if DEBUG:
        dbg0 = nc.dram_tensor("dbg0", [T, D], F32, kind="ExternalOutput")
        dbg1 = nc.dram_tensor("dbg1", [T, D], F32, kind="ExternalOutput")
        dbg2 = nc.dram_tensor("dbg2", [T, D], BF16, kind="ExternalOutput")
        dbg3 = nc.dram_tensor("dbg3", [T, D], BF16, kind="ExternalOutput")

    S = {}

    def sa(name, shape, dtype):
        S[name] = nc.alloc_sbuf_tensor(name, shape, dtype)

    sa("x", [128, NT * 1024], F32)
    sa("xT", [128, DC * 2048], BF16)
    sa("bg", [128, NT * 1024], BF16)
    sa("dd", [128, 10240], BF16)
    sa("di", [128, 2048], I32)
    sa("dw", [128, 2048], F32)
    sa("ee", [128, 8192], BF16)
    sa("ff", [128, 4096], BF16)
    sa("vq", [128, NT * 130], BF16)
    sa("hh", [128, 4480], BF16)
    sa("t1", [128, 512], BF16)
    sa("bb", [128, 1024], BF16)
    sa("gu", [128, 1024], BF16)
    sa("st", [128, 72], F32)
    sa("bn", [128, 12], F32)
    sa("qb", [128, 24], F32)
    sa("hm", [128, 128], F32)
    sa("ms", [128, 32], F32)
    sa("r8", [128, 8], F32)
    sa("g8", [1, 64], F32)
    sa("gi", [1, 4], I32)
    sa("tr", [128, 128], BF16)
    sa("pm", [128, 128], BF16)
    sa("rw", [128, 1], BF16)

    pS = nc.alloc_psum_tensor("pS", [128, 2048], F32)
    pAe = nc.alloc_psum_tensor("pAe", [128, 1024], F32)
    pAo = nc.alloc_psum_tensor("pAo", [128, 1024], F32)

    sem_names = [
        "ct", "cv", "cs", "cg",
        "dx", "dcst", "dcs2", "debv", "dmpr",
        "dwr0", "dwr1", "dowt", "dwsel", "dbrow",
        "dxho", "dxtr", "dpark", "dxp",
        "dwfA", "dwfB", "dhsp", "dhrl", "dhr2", "dhqo", "dhq2", "dhtr",
        "dwd", "dout", "dqvb",
    ]
    sem_cm, sems = [], {}
    for s in sem_names:
        cm = nc.semaphore(s)
        sems[s] = cm.__enter__()
        sem_cm.append(cm)

    blk_cm = nc.Block()
    block = blk_cm.__enter__()

    MSC = dict(eighth=0, eps=1, mhalf=2, invD=3, c127=4, tiny=5, invWN=6,
               mone=7, swg=8, swu=9, swd=10, accg=11, accu=12, accd=13,
               tmpa=14, i127=15)
    ISX0 = 16

    def program(w: Wk):
        mk = {}

        def msl(key):
            return S["ms"][:, MSC[key]:MSC[key] + 1]

        def vchain(*fns):
            m_ = 0
            for fn in fns:
                m_ = w.do("vector", fn, inc="cv")
                w.wait("vector", "cv", m_)
            return m_

        for g in range(4):
            src = bass.AP(x_in, g * 4 * 128 * 1024,
                          [[1024, 128], [128 * 1024, 4], [1, 1024]])
            w.dma("sync", S["x"][:, g * 4096:(g + 1) * 4096], src, "dx")
        mk["x_ld"] = w.v("dx")

        w.dma("sync", S["g8"][0:1, 0:8], logit8.ap(), "dcst")
        w.dma("sync", S["g8"][0:1, 8:16], u8.ap(), "dcst")
        w.dma("sync", S["g8"][0:1, 16:24], iota8.ap(), "dcst")
        w.dma("sync", S["qb"][:], qkvb0.ap(), "dcst")
        mk["cst"] = w.v("dcst")
        w.dma("gpsimd", S["tr"][:], tri_c.ap(), "dcs2")
        w.dma("gpsimd", S["pm"][:], p128c.ap(), "dcs2")
        w.dma("gpsimd", S["rw"][0:64, :], rpew.ap(), "dcs2")
        w.dma("gpsimd", S["dd"][0:64, 0:4096], relpT.ap(), "dcs2")
        mk["cst2"] = w.v("dcs2")

        for k2, vl in [("eighth", 0.125), ("eps", EPS), ("mhalf", -0.5),
                       ("invD", 1.0 / D), ("c127", 127.0), ("tiny", 1e-5),
                       ("invWN", 1.0 / (D * INNER)), ("mone", -1.0),
                       ("i127", 1.0 / 127.0)]:
            w.do("vector", lambda e, kk=k2, v=vl: e.memset(msl(kk), v))
        w.do("vector", lambda e: e.memset(S["vq"][:], 1.0))
        w.do("vector", lambda e: e.memset(S["bg"][0:1, 4096:4608], 0.0),
             inc="cv")
        mk["consts"] = w.v("cv")

        w.wait("scalar", "dcst", mk["cst"])
        w.wait("scalar", "cv", mk["consts"])
        _m = w.do("scalar", lambda e: e.activation(
            out=S["g8"][0:1, 24:32], in_=S["g8"][0:1, 8:16], func=AF.Ln),
            inc="cs")
        w.wait("scalar", "cs", _m)
        w.do("scalar", lambda e: e.activation(
            out=S["g8"][0:1, 32:40], in_=S["g8"][0:1, 24:32], func=AF.Ln,
            scale=msl("mone")[0:1, :]), inc="cs")
        mk["gum_s"] = w.v("cs")
        w.wait("vector", "cs", mk["gum_s"])
        def vstep(fn):
            m_ = w.do("vector", fn, inc="cv")
            w.wait("vector", "cv", m_)
        vstep(lambda e: e.tensor_tensor(
            out=S["g8"][0:1, 40:48], in0=S["g8"][0:1, 0:8],
            in1=S["g8"][0:1, 32:40], op=ALU.subtract))
        vstep(lambda e: e.reduce_max(
            out=S["g8"][0:1, 48:49], in_=S["g8"][0:1, 40:48], axis=AX.X))
        vstep(lambda e: e.tensor_tensor(
            out=S["g8"][0:1, 56:64], in0=S["g8"][0:1, 40:48],
            in1=bass.AP(S["g8"], 48, [[64, 1], [0, 8]]), op=ALU.is_ge))
        vstep(lambda e: e.tensor_tensor(
            out=S["g8"][0:1, 56:64], in0=S["g8"][0:1, 56:64],
            in1=S["g8"][0:1, 16:24], op=ALU.mult))
        vstep(lambda e: e.reduce_sum(
            out=S["g8"][0:1, 49:50], in_=S["g8"][0:1, 56:64], axis=AX.X))
        w.do("vector", lambda e: e.tensor_copy(
            out=S["gi"][0:1, 0:1], in_=S["g8"][0:1, 49:50]), inc="cv")
        mk["idx"] = w.v("cv")

        w.wait("tensor", "dcs2", mk["cst2"])
        for c in range(DC):
            w.do("tensor", lambda e, c=c: e.matmul(
                pS[0:1, 0:512], S["rw"][0:64, :],
                S["dd"][0:64, c * 512:(c + 1) * 512],
                start=True, stop=True), inc="ct")
            m = w.v("ct")
            w.wait("scalar", "ct", m)
            ms = w.do("scalar", lambda e, c=c: e.activation(
                out=S["bg"][0:1, c * 512:(c + 1) * 512], in_=pS[0:1, 0:512],
                func=AF.Exp), inc="cs")
            w.wait("tensor", "cs", ms)
        mk["ebv_s"] = w.v("cs")
        mk["bv_mm"] = w.v("ct")
        w.wait("sync", "cs", mk["ebv_s"])
        w.wait("sync", "cv", mk["consts"])
        w.dma("sync", bass.AP(ebv_h, 512, [[4096, 1], [1, 4096]]),
              S["bg"][0:1, 0:4096], "debv")
        w.dma("sync", bass.AP(ebv_h, 0, [[512, 1], [1, 512]]),
              S["bg"][0:1, 4096:4608], "debv")
        mk["ebv_out"] = w.v("debv")
        w.wait("sync", "debv", mk["ebv_out"])
        w.dma("sync", S["xT"][:, 0:4480],
              bass.AP(ebv_h, 0, [[1, 128], [1, 4480]]), "dmpr")
        mk["mpr"] = w.v("dmpr")
        w.wait("vector", "dmpr", mk["mpr"])
        w.do("vector", lambda e: e.tensor_copy(
            out=S["hh"][:, 0:4480], in_=S["xT"][:, 0:4480][:, ::-1]),
            inc="cv")
        mk["mflip"] = w.v("cv")

        pln_state = {"sqred": 0}
        passA = {"steps": [], "mpa": 0, "vred": 0, "tt": 0}
        for _mi, _mat in enumerate((wgT, wuT, wdT)):
            passA["steps"].append(("memset", _mi, _mat))
            for _pc in range(32):
                passA["steps"].append(("piece", _mi, _pc, _mat))

        def passA_pump(n):
            for _ in range(n):
                if not passA["steps"]:
                    return
                stp = passA["steps"].pop(0)
                if stp[0] == "memset":
                    _, mi, _m = stp
                    mm_ = w.do("vector", lambda e, mi=mi: e.memset(
                        S["st"][:, 68 + mi:69 + mi], 0.0), inc="cv")
                    w.wait("vector", "cv", mm_)
                elif stp[0] == "piece":
                    _, mi, pc, mat = stp
                    if mat is wdT:
                        src = bass.AP(wdT, pc * 128 * 1024,
                                      [[1024, 128], [1, 1024]])
                    else:
                        src = bass.AP(mat,
                                      (pc // 4) * 128 * 4096 +
                                      (pc % 4) * 1024,
                                      [[4096, 128], [1, 1024]])
                    w.wait("gpsimd", "cv", passA["vred"])
                    w.dma("gpsimd", S["dw"][:, 0:1024], src, "dwfA")
                    mA = w.v("dwfA")
                    w.wait("vector", "dwfA", mA)
                    vv = vchain(
                        lambda e: e.tensor_reduce(
                            out=S["st"][:, 71:72], in_=S["dw"][:, 0:1024],
                            axis=AX.X, op=ALU.add,
                            apply_absolute_value=True),
                        lambda e, mi=mi: e.tensor_tensor(
                            out=S["st"][:, 68 + mi:69 + mi],
                            in0=S["st"][:, 68 + mi:69 + mi],
                            in1=S["st"][:, 71:72], op=ALU.add))
                    passA["vred"] = vv
                    passA["mpa"] = vv

        guard = dict(
            bg_free=("dcst", 0),
            xh_free=("dcst", 0),
            xT_free=("cv", mk["mflip"]),
            ee_free=("ct", 0),
            bb_free=("cv", 0),
            pS_free=("cs", mk["ebv_s"]),
            pA_free=[("cv", 0), ("cv", 0)],
        )
        guard["bg_free"] = ("debv", mk["ebv_out"])


        def ln_pre(x_ready, quant=False, pre_wait=None):
            w.wait("vector", "dx", mk["x_ld"])
            w.wait("vector", "cv", x_ready)
            if pre_wait is not None:
                w.wait("vector", pre_wait[0], pre_wait[1])
            w.wait("scalar", "dx", mk["x_ld"])
            w.wait("scalar", "cv", x_ready)
            if pre_wait is not None:
                w.wait("scalar", pre_wait[0], pre_wait[1])
            sq = S["dw"][:, 1024:2048]
            for t in range(NT):
                xt = S["x"][:, t * 1024:(t + 1) * 1024]
                w.wait("scalar", "cv", w.v("cv"))
                msq = w.do("scalar", lambda e, xt=xt, sq=sq: e.activation(
                    out=sq, in_=xt, func=AF.Square), inc="cs")
                w.do("vector", lambda e, xt=xt, t=t: e.reduce_sum(
                    out=S["st"][:, t:t + 1], in_=xt, axis=AX.X), inc="cv")
                w.wait("vector", "cs", msq)
                w.do("vector", lambda e, t=t, sq=sq: e.reduce_sum(
                    out=S["st"][:, 16 + t:17 + t], in_=sq, axis=AX.X),
                    inc="cv")
            w.wait("vector", "cv", w.v("cv"))
            mveps = vchain(
                lambda e: e.tensor_scalar(
                    out=S["st"][:, 0:16], in0=S["st"][:, 0:16],
                    scalar1=msl("invD"), scalar2=None, op0=ALU.mult),
                lambda e: e.tensor_scalar(
                    out=S["st"][:, 16:32], in0=S["st"][:, 16:32],
                    scalar1=msl("invD"), scalar2=None, op0=ALU.mult),
                lambda e: e.tensor_tensor(
                    out=S["st"][:, 48:64], in0=S["st"][:, 0:16],
                    in1=S["st"][:, 0:16], op=ALU.mult),
                lambda e: e.tensor_tensor(
                    out=S["st"][:, 16:32], in0=S["st"][:, 16:32],
                    in1=S["st"][:, 48:64], op=ALU.subtract),
                lambda e: e.tensor_scalar(
                    out=S["st"][:, 32:48], in0=S["st"][:, 16:32],
                    scalar1=msl("eps"), scalar2=None, op0=ALU.add))
            w.wait("scalar", "cv", mveps)
            _m = w.do("scalar", lambda e: e.activation(
                out=S["st"][:, 48:64], in_=S["st"][:, 32:48], func=AF.Ln),
                inc="cs")
            w.wait("scalar", "cs", _m)
            w.do("scalar", lambda e: e.activation(
                out=S["st"][:, 32:48], in_=S["st"][:, 48:64], func=AF.Exp,
                scale=msl("mhalf")), inc="cs")
            mr_s = w.v("cs")
            w.wait("vector", "cs", mr_s)
            mmr = w.do("vector", lambda e: e.tensor_tensor(
                out=S["st"][:, 48:64], in0=S["st"][:, 0:16],
                in1=S["st"][:, 32:48], op=ALU.mult), inc="cv")
            w.wait("vector", "cv", mmr)
            bgs, bgm = guard["bg_free"]
            w.wait("vector", bgs, bgm)
            apm = []
            for t in range(NT):
                xt = S["x"][:, t * 1024:(t + 1) * 1024]
                bt = S["bg"][:, t * 1024:(t + 1) * 1024]
                if not quant:
                    w.do("vector", lambda e, xt=xt, bt=bt, t=t:
                         e.tensor_scalar(
                             out=bt, in0=xt,
                             scalar1=S["st"][:, 32 + t:33 + t],
                             scalar2=S["st"][:, 48 + t:49 + t],
                             op0=ALU.mult, op1=ALU.subtract), inc="cv")
                else:
                    vchain(
                        lambda e, xt=xt, t=t: e.tensor_scalar(
                            out=S["dw"][:, 0:1024], in0=xt,
                            scalar1=S["st"][:, 32 + t:33 + t],
                            scalar2=S["st"][:, 48 + t:49 + t],
                            op0=ALU.mult, op1=ALU.subtract),
                        lambda e: e.tensor_reduce(
                            out=S["st"][:, 64:65], in_=S["dw"][:, 0:1024],
                            axis=AX.X, op=ALU.max,
                            apply_absolute_value=True),
                        lambda e: e.tensor_scalar(
                            out=S["st"][:, 64:65], in0=S["st"][:, 64:65],
                            scalar1=msl("tiny"), scalar2=None, op0=ALU.max),
                        lambda e: e.reciprocal(
                            out=S["st"][:, 65:66], in_=S["st"][:, 64:65]),
                        lambda e, t=t: e.tensor_scalar(
                            out=S["ms"][:, ISX0 + t:ISX0 + t + 1],
                            in0=S["st"][:, 64:65], scalar1=msl("i127"),
                            scalar2=None, op0=ALU.mult),
                        lambda e, t=t: e.tensor_scalar(
                            out=S["st"][:, 66:67], in0=S["st"][:, 65:66],
                            scalar1=msl("c127"), scalar2=None,
                            op0=ALU.mult),
                        lambda e: e.tensor_scalar(
                            out=S["di"][:, 0:1024], in0=S["dw"][:, 0:1024],
                            scalar1=S["st"][:, 66:67], scalar2=None,
                            op0=ALU.mult),
                        lambda e, bt=bt: e.tensor_copy(
                            out=bt, in_=S["di"][:, 0:1024]))
                apm.append(w.v("cv"))
            xhs, xhm = guard["xh_free"]
            w.wait("sync", xhs, xhm)
            for t in range(NT):
                w.wait("sync", "cv", apm[t])
                w.dma("sync", bass.AP(xh_h, t * 128 * 1024,
                                      [[1024, 128], [1, 1024]]),
                      S["bg"][:, t * 1024:(t + 1) * 1024], "dxho")
            mout = w.v("dxho")
            guard["bg_free"] = ("dxho", mout)
            xts, xtm = guard["xT_free"]
            w.wait("sync", xts, xtm)
            w.wait("sync", "dxho", mout)
            for c in range(DC):
                w.dma("sync", S["xT"][:, c * 2048:(c + 1) * 2048],
                      bass.AP(xh_h, c * 128, [[1024, T], [1, 128]]),
                      "dxtr", transpose=True)
            mtr = w.v("dxtr")
            guard["xh_free"] = ("dxtr", mtr)
            return dict(xtr=mtr, apm=apm, xho=mout)

        def post_ln_tile(t, src_ap, sq_wait=None, probe=False):
            sq = S["dw"][:, 1024:2048]
            w.wait("scalar", "cv", pln_state["sqred"])
            if sq_wait is not None:
                w.wait("scalar", sq_wait[0], sq_wait[1])
            if sq_wait is not None:
                w.wait("vector", sq_wait[0], sq_wait[1])
            w.wait("vector", "cv", w.v("cv"))
            mra = vchain(lambda e, src_ap=src_ap: e.reduce_sum(
                out=S["st"][:, 64:65], in_=src_ap, axis=AX.X))
            w.wait("scalar", "cv", mra)
            msq = w.do("scalar", lambda e, src_ap=src_ap, sq=sq:
                       e.activation(out=sq, in_=src_ap, func=AF.Square),
                       inc="cs")
            if probe and PHASE == "plnA":
                w.wait("vector", "cs", msq)
                return mra
            w.wait("vector", "cs", msq)
            mv = vchain(
                lambda e, sq=sq: e.reduce_sum(
                    out=S["st"][:, 65:66], in_=sq, axis=AX.X),
                lambda e: e.tensor_scalar(
                    out=S["st"][:, 64:65], in0=S["st"][:, 64:65],
                    scalar1=msl("invD"), scalar2=None, op0=ALU.mult),
                lambda e: e.tensor_scalar(
                    out=S["st"][:, 65:66], in0=S["st"][:, 65:66],
                    scalar1=msl("invD"), scalar2=None, op0=ALU.mult),
                lambda e: e.tensor_tensor(
                    out=S["st"][:, 67:68], in0=S["st"][:, 64:65],
                    in1=S["st"][:, 64:65], op=ALU.mult),
                lambda e: e.tensor_tensor(
                    out=S["st"][:, 65:66], in0=S["st"][:, 65:66],
                    in1=S["st"][:, 67:68], op=ALU.subtract),
                lambda e: e.tensor_scalar(
                    out=S["st"][:, 66:67], in0=S["st"][:, 65:66],
                    scalar1=msl("eps"), scalar2=None, op0=ALU.add))
            pln_state["sqred"] = mv - 5
            if probe and PHASE == "plnB":
                return mv
            w.wait("scalar", "cv", mv)
            _m = w.do("scalar", lambda e: e.activation(
                out=S["st"][:, 67:68], in_=S["st"][:, 66:67], func=AF.Ln),
                inc="cs")
            w.wait("scalar", "cs", _m)
            w.do("scalar", lambda e: e.activation(
                out=S["st"][:, 66:67], in_=S["st"][:, 67:68], func=AF.Exp,
                scale=msl("mhalf")), inc="cs")
            mr = w.v("cs")
            w.wait("vector", "cs", mr)
            if probe and PHASE == "plnC":
                return mv
            mmr2 = w.do("vector", lambda e: e.tensor_tensor(
                out=S["st"][:, 67:68], in0=S["st"][:, 64:65],
                in1=S["st"][:, 66:67], op=ALU.mult), inc="cv")
            w.wait("vector", "cv", mmr2)
            mtsp = w.do("vector", lambda e: e.tensor_scalar(
                out=src_ap, in0=src_ap, scalar1=S["st"][:, 66:67],
                scalar2=S["st"][:, 67:68], op0=ALU.mult, op1=ALU.subtract),
                inc="cv")
            w.wait("vector", "cv", mtsp)
            if probe and PHASE == "plnD":
                return mtsp
            xt = S["x"][:, t * 1024:(t + 1) * 1024]
            w.do("vector", lambda e, xt=xt: e.tensor_tensor(
                out=xt, in0=xt, in1=src_ap, op=ALU.add), inc="cv")
            return w.v("cv")

        def bias_broadcast(brow_mark):
            w.wait("tensor", "dbrow", brow_mark)
            w.wait("tensor", "dcs2", mk["cst2"])
            ps_s, ps_m = guard["pS_free"]
            w.wait("tensor", ps_s, ps_m)
            for f in range(2):
                w.do("tensor", lambda e, f=f: e.matmul(
                    pS[:, f * 512:(f + 1) * 512], S["tr"][0:1, :],
                    S["bb"][0:1, f * 512:(f + 1) * 512],
                    start=True, stop=True), inc="ct")
            m = w.v("ct")
            w.wait("vector", "ct", m)
            bbs, bbm = guard["bb_free"]
            w.wait("vector", bbs, bbm)
            w.do("vector", lambda e: e.tensor_copy(
                out=S["bb"][:, 0:1024], in_=pS[:, 0:1024]), inc="cv")
            mb = w.v("cv")
            guard["pS_free"] = ("cv", mb)
            return mb

        def proj_phase(rhs_tensor_name, rhs_ready, lnxtr, mbb, prev_res):
            res = []
            for t in range(NT):
                half = (t % 2) * 1024
                if t < 2:
                    ps_s, ps_m = guard["pS_free"]
                    w.wait("tensor", ps_s, ps_m)
                    if prev_res is not None:
                        w.wait("tensor", "cv", prev_res)
                else:
                    w.wait("tensor", "cv", res[t - 2])
                if t == 0:
                    w.wait("tensor", rhs_ready[0], rhs_ready[1])
                    w.wait("tensor", "dxtr", lnxtr)
                mmm = None
                for f2 in range(2):
                    for c in range(DC):
                        last = (f2 == 1 and c == DC - 1)
                        mmm = w.do("tensor",
                                   lambda e, t=t, f2=f2, c=c, half=half:
                                   e.matmul(
                                       pS[:, half + f2 * 512:
                                          half + (f2 + 1) * 512],
                                       S["xT"][:, c * 2048 + t * 128:
                                               c * 2048 + (t + 1) * 128],
                                       bass.AP(S["ee"], c * 1024 + f2 * 512,
                                               [[8192, 128], [1, 512]]),
                                       start=(c == 0), stop=(c == DC - 1)),
                                   inc="ct" if last else None) or mmm
                w.wait("vector", "ct", mmm)
                if mbb is not None:
                    if t == 0:
                        w.wait("vector", "cv", mbb)
                    w.do("vector", lambda e, half=half: e.tensor_tensor(
                        out=pS[:, half:half + 1024],
                        in0=pS[:, half:half + 1024],
                        in1=S["bb"][:, 0:1024], op=ALU.add), inc="cv")
                if rhs_ready[0] == "dowt" and PHASE == "opjb%d" % t:
                    return w.v("cv")
                pb = (rhs_ready[0] == "dowt" and t == 0 and
                      PHASE.startswith("pln"))
                res.append(post_ln_tile(t, pS[:, half:half + 1024],
                                        sq_wait=("ct", mmm), probe=pb))
                if pb:
                    return res[0]
                if rhs_ready[0] == "dowt" and PHASE == "opjt%d" % t:
                    return res[t]
            guard["pS_free"] = ("cv", res[-1])
            guard["ee_free"] = ("ct", w.v("ct"))
            guard["bb_free"] = ("cv", w.v("cv"))
            return res[-1]

        def attn_layer(L, lnmk):
            lw = qkvw[L]
            ees, eem = guard["ee_free"]
            w.wait("gpsimd", ees, eem)
            w.dma("gpsimd",
                  bass.AP(S["ee"], 0, [[8192, 128], [1024, 8], [1, 1024]]),
                  bass.AP(oww[L], 0, [[1024, 128], [131072, 8], [1, 1024]]),
                  "dowt")
            mk["owt"] = w.v("dowt")
            bbs, bbm = guard["bb_free"]
            w.wait("gpsimd", bbs, bbm)
            w.dma("gpsimd", S["bb"][0:1, 0:1024], obb[L].ap(), "dbrow")
            mk["brow"] = w.v("dbrow")
            dib = S["dw"].ap().bitcast(BF16)
            if L == 0:
                w.dma("gpsimd", dib[0:1, 0:1024], qvb0.ap(), "dqvb")
                mk["qvb"] = w.v("dqvb")
            if L == 1:
                w.wait("gpsimd", "cv", mk["l0_core_v"])
                w.dma("gpsimd", S["hh"][:, 0:2048], cosTc.ap(), "dcs2")
                w.dma("gpsimd", S["hh"][:, 2048:4096], sinTc.ap(), "dcs2")
                mk["cossin"] = w.v("dcs2")

            if L == 0 and PHASE == "pre0":
                w.wait("sync", "dowt", mk["owt"])
                w.wait("sync", "dbrow", mk["brow"])
                w.wait("sync", "dqvb", mk.get("qvb", 0))
                return 0
            last_exp = {0: 0, 1: 0}
            last_eb = {0: 0, 1: 0}
            last_at_ct = {0: 0, 1: 0}
            av_done = {0: {0: 0, 1: 0}, 1: {0: 0, 1: 0}}
            pa_ev = {0: 0, 1: 0}
            wr0init = mk["bv_mm"] if L == 0 else w.v("ct")
            wr_free = {0: wr0init, 1: wr0init}
            last_evac_v = 0

            for duo in range(8):
                slot = duo % 2
                wsem = "dwr0" if slot == 0 else "dwr1"
                w.wait("gpsimd", "ct", wr_free[slot])
                if duo == 0 and L == 0:
                    w.wait("gpsimd", "ct", mk["bv_mm"])
                for bi, base in enumerate((0, D, 2 * D)):
                    src = bass.AP(lw, base + duo * 128,
                                  [[3072, 128], [3072 * 128, 8], [1, 128]])
                    dst = bass.AP(S["dd"], slot * 3072 + bi * 128,
                                  [[10240, 128], [384, 8], [1, 128]])
                    w.dma("gpsimd", dst, src, wsem)
                mwr = w.v(wsem)
                if L == 1:
                    passA_pump(13)

                w.wait("tensor", wsem, mwr)
                if duo == 0:
                    w.wait("tensor", "dxtr", lnmk["xtr"])
                    ps_s, ps_m = guard["pS_free"]
                    w.wait("tensor", ps_s, ps_m)
                else:
                    w.wait("tensor", "cs", last_exp[0])
                    w.wait("tensor", "cs", last_exp[1])
                wr0 = slot * 3072
                qev = 0
                qev_h = {}
                mm = 0
                for bi in (0, 1):
                    for fs in range(4):
                        if bi == 1:
                            w.wait("tensor", "cv", qev_h[fs])
                        for c in range(DC):
                            mm = w.do("tensor", lambda e, bi=bi, fs=fs, c=c:
                                      e.matmul(
                                          pS[:, fs * 512:(fs + 1) * 512],
                                          bass.AP(S["dd"],
                                                  wr0 + c * 384 + bi * 128,
                                                  [[10240, 128], [1, 128]]),
                                          S["xT"][:, c * 2048 + fs * 512:
                                                  c * 2048 + (fs + 1) * 512],
                                          start=(c == 0),
                                          stop=(c == DC - 1)),
                                      inc="ct" if c == DC - 1 else None) or mm
                        dst = S["ff"][:, bi * 2048 + fs * 512:
                                      bi * 2048 + (fs + 1) * 512]
                        w.wait("vector", "ct", mm)
                        if L == 0:
                            w.do("vector", lambda e, dst=dst, fs=fs, bi=bi:
                                 e.tensor_scalar(
                                     out=dst,
                                     in0=pS[:, fs * 512:(fs + 1) * 512],
                                     scalar1=S["qb"][:, bi * 8 + duo:
                                                     bi * 8 + duo + 1],
                                     scalar2=None, op0=ALU.add), inc="cv")
                        else:
                            w.do("vector", lambda e, dst=dst, fs=fs:
                                 e.tensor_copy(
                                     out=dst,
                                     in_=pS[:, fs * 512:(fs + 1) * 512]),
                                 inc="cv")
                        qev = w.v("cv")
                        qev_h[fs] = qev
                if L == 0 and duo == 0 and PHASE == "qk0":
                    w.wait("sync", "dowt", mk["owt"])
                    w.wait("sync", "dbrow", mk["brow"])
                    w.wait("sync", "dqvb", mk.get("qvb", 0))
                    return qev
                w.wait("tensor", "cv", pa_ev[0])
                w.wait("tensor", "cv", pa_ev[1])
                mvq = 0
                if L == 0 and duo == 0:
                    w.wait("tensor", "dqvb", mk["qvb"])
                mm = 0
                mvq_h = {}
                for t in range(NT):
                    pv = pAe if t % 2 == 0 else pAo
                    if t >= 2:
                        w.wait("tensor", "cv", mvq_h[t - 2])
                    for c in range(DC):
                        lastv = (c == DC - 1 and L != 0)
                        mm = w.do("tensor", lambda e, t=t, c=c, pv=pv,
                                  lastv=lastv: e.matmul(
                                      pv[:, 0:128],
                                      S["xT"][:, c * 2048 + t * 128:
                                              c * 2048 + (t + 1) * 128],
                                      bass.AP(S["dd"], wr0 + c * 384 + 256,
                                              [[10240, 128], [1, 128]]),
                                      start=(c == 0), stop=lastv),
                                  inc="ct" if lastv else None) or mm
                    if L == 0:
                        mm = w.do("tensor", lambda e, pv=pv: e.matmul(
                            pv[:, 0:128], S["tr"][0:1, :],
                            dib[0:1, duo * 128:(duo + 1) * 128],
                            start=False, stop=True), inc="ct")
                    w.wait("vector", "ct", mm)
                    dst = bass.AP(S["vq"], t * 130,
                                  [[2080, 128], [65, 2], [1, 64]])
                    w.do("vector", lambda e, dst=dst, pv=pv:
                         e.tensor_copy(out=dst, in_=pv[:, 0:128]),
                         inc="cv")
                    mvq = w.v("cv")
                    mvq_h[t] = mvq
                wr_free[slot] = w.v("ct")

                if L == 0 and duo == 0 and PHASE == "qkv0":
                    return mvq
                if L == 1:
                    w.wait("gpsimd", "dcs2", mk["cossin"])
                    w.wait("vector", "dcs2", mk["cossin"])
                    w.wait("gpsimd", "cv", qev)
                    w.wait("tensor", "dcs2", mk["cst2"])
                    dif = S["di"].ap().bitcast(F32)
                    radd = {}
                    rsin = {}
                    for i in range(8):
                        bi, pc = divmod(i, 4)
                        s0 = bi * 2048 + pc * 512
                        c0 = pc * 512
                        sb = (i % 2) * 1024
                        pb = (i % 2) * 512
                        w.wait("gpsimd", "cv", radd.get(i - 2, 0))
                        mg = w.do("gpsimd", lambda e, s0=s0, c0=c0, sb=sb,
                                  dif=dif: e.tensor_tensor(
                                      out=dif[:, sb:sb + 512],
                                      in0=S["ff"][:, s0:s0 + 512],
                                      in1=S["hh"][:, c0:c0 + 512],
                                      op=ALU.mult), inc="cg")
                        w.wait("tensor", "cv", rsin.get(i - 2, 0))
                        w.do("tensor", lambda e, s0=s0, pb=pb: e.matmul(
                            pS[:, pb:pb + 512], S["pm"][:],
                            S["ff"][:, s0:s0 + 512],
                            start=True, stop=True), inc="ct")
                        mt = w.v("ct")
                        w.wait("vector", "ct", mt)
                        w.wait("vector", "cv", radd.get(i - 2, 0))
                        w.do("vector", lambda e, c0=c0, sb=sb, pb=pb,
                             dif=dif: e.tensor_tensor(
                                 out=dif[:, sb + 512:sb + 1024],
                                 in0=pS[:, pb:pb + 512],
                                 in1=S["hh"][:, 2048 + c0:2048 + c0 + 512],
                                 op=ALU.mult), inc="cv")
                        rsin[i] = w.v("cv")
                        w.wait("vector", "cv", rsin[i])
                        w.wait("vector", "cg", mg)
                        w.do("vector", lambda e, s0=s0, sb=sb, dif=dif:
                             e.tensor_tensor(
                                 out=S["ff"][:, s0:s0 + 512],
                                 in0=dif[:, sb:sb + 512],
                                 in1=dif[:, sb + 512:sb + 1024],
                                 op=ALU.add), inc="cv")
                        radd[i] = w.v("cv")
                    mk_rope = radd[7]
                else:
                    mk_rope = qev

                for qc in (0, 1024):
                    ks = list(range(16)) if L == 0 else \
                        list(range(0, qc // 128 + 8))

                    def do_av(kav, exp_mv, eb_mv):
                        for par in (0, 1):
                            pv = pAe if par == 0 else pAo
                            atb = S["dd"][:, 6144 +
                                          (par * 2 + kav % 2) * 1024:
                                          6144 +
                                          (par * 2 + kav % 2 + 1) * 1024]
                            if eb_mv[par]:
                                w.wait("tensor", "cv", eb_mv[par])
                            else:
                                w.wait("tensor", "cs", exp_mv[par])
                            if kav == ks[0]:
                                w.wait("tensor", "cv", pa_ev[par])
                            jlo = 0 if L == 0 else max(0, kav - qc // 128)
                            mav = 0
                            for j in range(jlo, 8):
                                klast = 15 if L == 0 else qc // 128 + j
                                mav = w.do("tensor",
                                           lambda e, j=j, k=kav, par=par,
                                           pv=pv, atb=atb, klast=klast:
                                           e.matmul(
                                               pv[:, j * 128:j * 128 + 65],
                                               atb[:, j * 128:(j + 1) * 128],
                                               bass.AP(S["vq"],
                                                       k * 130 + par * 65,
                                                       [[2080, 128],
                                                        [1, 65]]),
                                               start=(k == 0 and j % 4 == 0),
                                               stop=(k == klast),
                                               skip_group_check=True),
                                           inc="ct" if j == 7 else None)                                     or mav
                            av_done[kav % 2][par] = mav
                            last_at_ct[par] = mav

                    prevav = None
                    for k in ks:
                        exp_m = {}
                        eb_m = {}
                        mqk_p = {}
                        for par in (0, 1):
                            rl = par * 64
                            p0 = par * 1024
                            w.wait("tensor", "cs", last_exp[par])
                            if k == ks[0]:
                                w.wait("tensor", "cv", mk_rope)
                                w.wait("tensor", "cv", mvq)
                            mqk = 0
                            for f2 in range(2):
                                mqk = w.do("tensor",
                                           lambda e, k=k, qc=qc, f2=f2, rl=rl,
                                           p0=p0: e.matmul(
                                               pS[:, p0 + f2 * 512:
                                                  p0 + (f2 + 1) * 512],
                                               S["ff"][rl:rl + 64,
                                                       2048 + k * 128:
                                                       2048 + (k + 1) * 128],
                                               S["ff"][rl:rl + 64,
                                                       qc + f2 * 512:
                                                       qc + (f2 + 1) * 512],
                                               start=True, stop=True,
                                               tile_position=(rl, 0)),
                                           inc="ct" if f2 == 1 else None)                                     or mqk
                            mqk_p[par] = mqk
                        if prevav is not None:
                            do_av(*prevav)
                        for par in (0, 1):
                            rl = par * 64
                            p0 = par * 1024
                            mqk = mqk_p[par]
                            atb = S["dd"][:, 6144 +
                                          (par * 2 + k % 2) * 1024:
                                          6144 +
                                          (par * 2 + k % 2 + 1) * 1024]
                            w.wait("scalar", "ct", mqk)
                            w.wait("scalar", "ct", av_done[k % 2][par])
                            w.do("scalar", lambda e, atb=atb, p0=p0:
                                 e.activation(
                                     out=atb,
                                     in_=pS[:, p0:p0 + 1024],
                                     func=AF.Exp,
                                     scale=msl("eighth")), inc="cs")
                            exp_m[par] = w.v("cs")
                            last_exp[par] = exp_m[par]
                            if L == 0:
                                base = qc - k * 128 + 1920
                                w.wait("vector", "cs", exp_m[par])
                                w.wait("vector", "ct", av_done[k % 2][par])
                                w.do("vector",
                                     lambda e, atb=atb, base=base:
                                     e.tensor_tensor(
                                         out=atb, in0=atb,
                                         in1=S["hh"][:, base:base + 1024],
                                         op=ALU.mult), inc="cv")
                                eb_m[par] = w.v("cv")
                            else:
                                off = k * 128 - qc
                                if off < 0:
                                    eb_m[par] = 0
                                else:
                                    w.wait("vector", "cs", exp_m[par])
                                    w.wait("vector", "ct",
                                           av_done[k % 2][par])
                                    w.do("vector",
                                         lambda e, atb=atb, off=off:
                                         e.tensor_tensor(
                                             out=atb[:, off:off + 128],
                                             in0=atb[:, off:off + 128],
                                             in1=S["tr"][:],
                                             op=ALU.mult), inc="cv")
                                    eb_m[par] = w.v("cv")
                        prevav = (k, dict(exp_m), dict(eb_m))
                    do_av(*prevav)
                    for par in (0, 1):
                        h = 2 * duo + par
                        pv = pAe if par == 0 else pAo
                        w.wait("vector", "ct", last_at_ct[par])
                        bgs, bgm = guard["bg_free"]
                        if duo == 0 and qc == 0:
                            w.wait("vector", bgs, bgm)
                        w.wait("vector", "cv", w.v("cv"))
                        mrc = w.do("vector", lambda e, pv=pv: e.reciprocal(
                            out=S["r8"][:],
                            in_=bass.AP(pv, 64, [[1024, 128], [128, 8]])),
                            inc="cv")
                        w.wait("vector", "cv", mrc)
                        mev = 0
                        for j in range(8):
                            t = (qc + j * 128) // 128
                            mev = w.do("vector",
                                       lambda e, pv=pv, j=j, t=t, h=h:
                                       e.tensor_scalar(
                                           out=S["bg"][:, t * 1024 + h * 64:
                                                       t * 1024 +
                                                       (h + 1) * 64],
                                           in0=pv[:, j * 128:j * 128 + 64],
                                           scalar1=S["r8"][:, j:j + 1],
                                           scalar2=None, op0=ALU.mult),
                                       inc="cv" if j == 7 else None) or mev
                        pa_ev[par] = mev
                        last_evac_v = mev
                    if L == 0 and duo == 0 and qc == 1024 and                             PHASE == "core0":
                        return last_evac_v
                    if L == 0 and qc == 1024 and                             PHASE == "core0d%d" % duo:
                        return last_evac_v

            if L == 0 and PHASE == "core0z":
                return last_evac_v
            if L == 0:
                mk["l0_core_v"] = last_evac_v
            guard["pS_free"] = ("cs", max(last_exp[0], last_exp[1]))
            guard["pA_free"] = [("cv", pa_ev[0]), ("cv", pa_ev[1])]

            xhs, xhm = guard["xh_free"]
            w.wait("sync", xhs, xhm)
            w.wait("sync", "cv", last_evac_v)
            for t in range(NT):
                w.dma("sync", bass.AP(xh_h, t * 128 * 1024,
                                      [[1024, 128], [1, 1024]]),
                      S["bg"][:, t * 1024:(t + 1) * 1024], "dxho")
            mo = w.v("dxho")
            guard["bg_free"] = ("dxho", mo)
            w.wait("sync", "dxho", mo)
            if DEBUG and L == 0:
                for t in range(NT):
                    w.dma("sync", bass.AP(dbg2, t * 128 * 1024,
                                          [[1024, 128], [1, 1024]]),
                          S["bg"][:, t * 1024:(t + 1) * 1024], "dout")
            w.wait("sync", "ct", wr_free[1])
            for c in range(DC):
                w.dma("sync", S["xT"][:, c * 2048:(c + 1) * 2048],
                      bass.AP(xh_h, c * 128, [[1024, T], [1, 128]]),
                      "dxtr", transpose=True)
            motr = w.v("dxtr")
            guard["xh_free"] = ("dxtr", motr)
            guard["xT_free"] = ("ct", None)

            if L == 0 and PHASE == "otr0":
                w.wait("sync", "dxtr", motr)
                return 0
            mbb = bias_broadcast(mk["brow"])
            if L == 0 and PHASE == "obb0":
                return mbb
            res = proj_phase("ee", ("dowt", mk["owt"]), motr, mbb, None)
            guard["xT_free"] = ("ct", w.v("ct"))
            mk["otr"] = motr
            return res

        def l0ffn(x_ready):
            ees, eem = guard["ee_free"]
            w.wait("gpsimd", ees, eem)
            w.wait("gpsimd", "cv", mk["idx"])
            bbs, bbm = guard["bb_free"]
            w.wait("gpsimd", bbs, bbm)
            if w.cur == "gpsimd":
                with w.eng.register("ridx") as r, \
                        w.eng.register("roff") as ro:
                    w.eng.reg_load(r, S["gi"][0:1, 0:1])
                    w.eng.reg_mul(ro, r, D * D)
                    w.eng.dma_start(
                        out=bass.AP(S["ee"], 0,
                                    [[8192, 128], [1024, 8], [1, 1024]]),
                        in_=bass.AP(Wexp, ro,
                                    [[1024, 128], [131072, 8], [1, 1024]])
                    ).then_inc(w.sem["dwsel"], 16)
                    w.eng.reg_mul(ro, r, D)
                    w.eng.dma_start(
                        out=S["bb"][0:1, 0:1024],
                        in_=bass.AP(bexp, ro, [[1024, 1], [1, 1024]])
                    ).then_inc(w.sem["dbrow"], 16)
            w.bump("dwsel", 16)
            w.bump("dbrow", 16)
            msel = w.v("dwsel")
            mbrow = w.v("dbrow")

            lnmk = ln_pre(x_ready)
            mbb = bias_broadcast(mbrow)
            res = proj_phase("ee", ("dwsel", msel), lnmk["xtr"], mbb,
                             x_ready)
            guard["xT_free"] = ("ct", w.v("ct"))
            return res

        def l1ffn(x_ready):
            passA_pump(10 ** 6)
            mpa = passA["mpa"]
            w.wait("gpsimd", "cv", mpa)
            w.do("gpsimd", lambda e: e.memset(S["dw"][:, 0:128], 1.0),
                 inc="cg")
            mones = w.v("cg")
            w.wait("tensor", "cg", mones)
            w.wait("tensor", "cv", mpa)
            ps_s0, ps_m0 = guard["pS_free"]
            w.wait("tensor", ps_s0, ps_m0)
            mbc = w.do("tensor", lambda e: e.matmul(
                pS[:, 0:3], S["dw"][:, 0:128], S["st"][:, 68:71],
                start=True, stop=True), inc="ct")
            w.wait("vector", "ct", mbc)
            msw = vchain(
                lambda e: e.tensor_copy(
                    out=S["ms"][:, MSC["accg"]:MSC["accg"] + 3],
                    in_=pS[:, 0:3]),
                lambda e: e.tensor_scalar(
                    out=S["ms"][:, MSC["accg"]:MSC["accg"] + 3],
                    in0=S["ms"][:, MSC["accg"]:MSC["accg"] + 3],
                    scalar1=msl("invWN"), scalar2=None, op0=ALU.mult),
                lambda e: e.tensor_scalar(
                    out=S["ms"][:, MSC["accg"]:MSC["accg"] + 3],
                    in0=S["ms"][:, MSC["accg"]:MSC["accg"] + 3],
                    scalar1=msl("tiny"), scalar2=None, op0=ALU.max),
                lambda e: e.reciprocal(
                    out=S["ms"][:, MSC["swg"]:MSC["swg"] + 3],
                    in_=S["ms"][:, MSC["accg"]:MSC["accg"] + 3]),
                lambda e: e.tensor_scalar(
                    out=S["ms"][:, MSC["accg"]:MSC["accg"] + 3],
                    in0=S["ms"][:, MSC["accg"]:MSC["accg"] + 3],
                    scalar1=0.5, scalar2=None, op0=ALU.mult))
            guard["pS_free"] = ("cv", w.v("cv"))

            lnmk = ln_pre(x_ready, quant=True, pre_wait=("ct", mbc))

            w.wait("sync", "cv", lnmk["apm"][-1])
            for t in range(NT):
                w.dma("sync", bass.AP(xp_h, t * 128 * 1024,
                                      [[1024, 128], [1, 1024]]),
                      S["x"][:, t * 1024:(t + 1) * 1024], "dpark")
            mpark = w.v("dpark")

            w.wait("vector", "cv", msw)
            isx = S["ms"][:, ISX0:ISX0 + 16]
            mscale = vchain(
                lambda e: e.reciprocal(out=msl("tmpa"), in_=msl("swg")),
                lambda e: e.tensor_tensor(
                    out=S["st"][:, 0:16], in0=isx,
                    in1=bass.AP(S["ms"], MSC["tmpa"], [[32, 128], [0, 16]]),
                    op=ALU.mult),
                lambda e: e.reciprocal(out=msl("tmpa"), in_=msl("swu")),
                lambda e: e.tensor_tensor(
                    out=S["st"][:, 16:32], in0=isx,
                    in1=bass.AP(S["ms"], MSC["tmpa"], [[32, 128], [0, 16]]),
                    op=ALU.mult))

            ees, eem = guard["ee_free"]
            res_ct = []
            prev_ic_ct = 0
            nspill = 0
            for ic in range(8):
                w.wait("vector", "cv", lnmk["apm"][-1])
                w.wait("vector", "ct", prev_ic_ct)
                if ic == 0:
                    w.wait("vector", ees, eem)
                for mi, mat in enumerate((wgT, wuT)):
                    for pi in range(2):
                        src = bass.AP(mat,
                                      pi * 4 * 128 * 4096 + ic * 512,
                                      [[4096, 128], [128 * 4096, 4],
                                       [1, 512]])
                        w.wait("gpsimd", "cv", w.v("cv"))
                        w.dma("gpsimd", S["dw"][:, 0:2048], src, "dwfB")
                        mB = w.v("dwfB")
                        w.wait("vector", "dwfB", mB)
                        w.wait("vector", "cv", msw)
                        thr = "accg" if mi == 0 else "accu"
                        dib2 = S["di"].ap().bitcast(BF16)
                        dst = bass.AP(S["ee"],
                                      mi * 4096 + pi * 2048,
                                      [[8192, 128], [1, 2048]])
                        thrB = bass.AP(S["ms"], MSC[thr],
                                       [[32, 128], [0, 2048]])
                        vchain(
                            lambda e, thrB=thrB: e.tensor_tensor(
                                out=dib2[:, 0:2048], in0=S["dw"][:, 0:2048],
                                in1=thrB, op=ALU.is_ge),
                            lambda e, thrB=thrB: e.scalar_tensor_tensor(
                                out=dib2[:, 2048:4096],
                                in0=S["dw"][:, 0:2048],
                                scalar=-1.0, in1=thrB,
                                op0=ALU.mult, op1=ALU.is_ge),
                            lambda e, dst=dst: e.tensor_tensor(
                                out=dst, in0=dib2[:, 0:2048],
                                in1=dib2[:, 2048:4096], op=ALU.subtract))
                mwq = w.v("cv")

                for t in range(NT):
                    half = (t % 2) * 1024
                    if ic == 0 and t < 2:
                        ps_s, ps_m = guard["pS_free"]
                        w.wait("tensor", ps_s, ps_m)
                    if t == 0:
                        w.wait("tensor", "cv", mwq)
                        if ic == 0:
                            w.wait("tensor", "dxtr", lnmk["xtr"])
                    if len(res_ct) >= 2:
                        w.wait("tensor", "cv", res_ct[-2])
                    mgu = None
                    for mi in range(2):
                        for c in range(DC):
                            last = (mi == 1 and c == DC - 1)
                            mgu = w.do("tensor",
                                       lambda e, t=t, mi=mi, c=c, half=half,
                                       ic=ic: e.matmul(
                                           pS[:, half + mi * 512:
                                              half + (mi + 1) * 512],
                                           S["xT"][:, c * 2048 + t * 128:
                                                   c * 2048 + (t + 1) * 128],
                                           bass.AP(S["ee"],
                                                   mi * 4096 + c * 512,
                                                   [[8192, 128], [1, 512]]),
                                           start=(c == 0),
                                           stop=(c == DC - 1)),
                                       inc="ct" if last else None) or mgu
                    prev = res_ct[-1] if res_ct else 0
                    guf = S["gu"].ap().bitcast(F32)
                    bgf = S["bg"].ap().bitcast(F32)
                    w.wait("scalar", "ct", mgu)
                    w.wait("scalar", "cv", mscale)
                    w.wait("scalar", "cv", prev)
                    mssi = w.do("scalar", lambda e, half=half, t=t, guf=guf:
                                e.activation(
                                    out=guf[:, 0:512],
                                    in_=pS[:, half:half + 512],
                                    func=AF.Silu,
                                    scale=S["st"][:, t:t + 1]), inc="cs")
                    w.wait("vector", "ct", mgu)
                    w.wait("vector", "cv", prev)
                    w.wait("vector", "cs", mssi)
                    if nspill >= 2:
                        w.wait("vector", "dhsp", 16 * nspill)
                    hslot = bgf[:, (t % 2) * 512:(t % 2) * 512 + 512]
                    mstt = w.do("vector", lambda e, half=half, t=t,
                                hslot=hslot, guf=guf: e.scalar_tensor_tensor(
                                    out=hslot,
                                    in0=pS[:, half + 512:half + 1024],
                                    scalar=S["st"][:, 16 + t:17 + t],
                                    in1=guf[:, 0:512],
                                    op0=ALU.mult, op1=ALU.mult), inc="cv")
                    w.wait("vector", "cv", mstt)
                    w.do("vector", lambda e, ic=ic, t=t, hslot=hslot:
                         e.tensor_reduce(
                             out=S["hm"][:, ic * 16 + t:ic * 16 + t + 1],
                             in_=hslot, axis=AX.X, op=ALU.max,
                             apply_absolute_value=True), inc="cv")
                    mh = w.v("cv")
                    res_ct.append(mh)
                    w.wait("sync", "cv", mh)
                    w.dma("sync",
                          bass.AP(h_h, t * 128 * INNER + ic * 512,
                                  [[INNER, 128], [1, 512]]),
                          hslot, "dhsp")
                    nspill += 1
                prev_ic_ct = w.v("ct")
            guard["ee_free"] = ("ct", w.v("ct"))

            m_gu_cg = w.v("cg")
            w.wait("vector", "cg", m_gu_cg)
            mrl = {}
            bgf2 = S["bg"].ap().bitcast(F32)
            rlm = {}
            qm = {}
            hqout = {}

            def issue_rl(i):
                t_, hp_ = divmod(i, 2)
                slot = i % 2
                eng = "sync" if slot == 0 else "gpsimd"
                sem = "dhrl" if slot == 0 else "dhr2"
                w.wait(eng, "dhsp", w.v("dhsp"))
                w.wait(eng, "cv", qm.get(i - 2, 0))
                w.dma(eng, bgf2[:, 1024 + slot * 2048:
                                3072 + slot * 2048],
                      bass.AP(h_h, t_ * 128 * INNER + hp_ * 2048,
                              [[INNER, 128], [1, 2048]]), sem)
                rlm[i] = w.v(sem)

            issue_rl(0)
            issue_rl(1)
            for t in range(NT):
                vchain(
                    lambda e, t=t: e.tensor_reduce(
                        out=S["st"][:, 64:65],
                        in_=bass.AP(S["hm"], t, [[128, 128], [16, 8]]),
                        axis=AX.X, op=ALU.max),
                    lambda e: e.tensor_scalar(
                        out=S["st"][:, 64:65], in0=S["st"][:, 64:65],
                        scalar1=msl("tiny"), scalar2=None, op0=ALU.max),
                    lambda e: e.reciprocal(
                        out=S["st"][:, 65:66], in_=S["st"][:, 64:65]),
                    lambda e: e.tensor_scalar(
                        out=S["st"][:, 66:67], in0=S["st"][:, 65:66],
                        scalar1=msl("c127"), scalar2=None, op0=ALU.mult))
                hqb = S["hh"] if t % 2 == 0 else S["dd"]
                hqsem = "dhqo" if t % 2 == 0 else "dhq2"
                w.wait("vector", hqsem, hqout.get(t - 2, 0))
                for hp in range(2):
                    i = t * 2 + hp
                    slot = i % 2
                    w.wait("vector", "dhrl" if slot == 0 else "dhr2",
                           rlm[i])
                    base = 1024 + slot * 2048
                    mq = vchain(
                        lambda e, base=base: e.tensor_scalar(
                            out=S["di"][:, 0:2048],
                            in0=bgf2[:, base:base + 2048],
                            scalar1=S["st"][:, 66:67], scalar2=None,
                            op0=ALU.mult),
                        lambda e, hp=hp, hqb=hqb: e.tensor_copy(
                            out=hqb[:, hp * 2048:(hp + 1) * 2048],
                            in_=S["di"][:, 0:2048]))
                    qm[i] = mq
                    if i + 2 < 2 * NT:
                        issue_rl(i + 2)
                mrl[t] = qm[t * 2 + 1]
                oeng = "sync" if t % 2 == 0 else "gpsimd"
                w.wait(oeng, "cv", qm[t * 2 + 1])
                w.dma(oeng,
                      bass.AP(hq_h, t * 128 * INNER, [[INNER, 128],
                                                      [1, 4096]]),
                      hqb[:, 0:4096], hqsem)
                hqout[t] = w.v(hqsem)
            mhq_all = w.v("dhqo")
            mhq_al2 = w.v("dhq2")

            accv = []
            prev_blk_v = 0
            first_pass_marks = [0] * NT
            for blk in range(4):
                w.wait("sync", "dhqo", mhq_all)
                w.wait("sync", "dhq2", mhq_al2)
                if blk:
                    w.wait("sync", "ct", mk.get("dn_ct", 0))
                for icl in range(8):
                    ic = blk * 8 + icl
                    if icl < 5:
                        dst = S["dd"][:, icl * 2048:(icl + 1) * 2048]
                    elif icl < 7:
                        dst = S["ff"][:, (icl - 5) * 2048:(icl - 4) * 2048]
                    else:
                        dst = S["hh"][:, 0:2048]
                    w.dma("sync", dst,
                          bass.AP(hq_h, ic * 128, [[INNER, T], [1, 128]]),
                          "dhtr", transpose=True)
                mtr = w.v("dhtr")
                w.wait("vector", "ct", mk.get("dn_ct", 0))
                if blk == 0:
                    ees2, eem2 = guard["ee_free"]
                    w.wait("vector", ees2, eem2)
                for icl in range(8):
                    ic = blk * 8 + icl
                    w.wait("gpsimd", "cv", w.v("cv"))
                    w.dma("gpsimd", S["dw"][:, 0:1024],
                          bass.AP(wdT, ic * 128 * 1024,
                                  [[1024, 128], [1, 1024]]), "dwd")
                    mw_ = w.v("dwd")
                    w.wait("vector", "dwd", mw_)
                    dib3 = S["di"].ap().bitcast(BF16)
                    dst = bass.AP(S["ee"], icl * 1024,
                                  [[8192, 128], [1, 1024]])
                    thrB = bass.AP(S["ms"], MSC["accd"],
                                   [[32, 128], [0, 1024]])
                    vchain(
                        lambda e, thrB=thrB: e.tensor_tensor(
                            out=dib3[:, 0:1024], in0=S["dw"][:, 0:1024],
                            in1=thrB, op=ALU.is_ge),
                        lambda e, thrB=thrB: e.scalar_tensor_tensor(
                            out=dib3[:, 1024:2048], in0=S["dw"][:, 0:1024],
                            scalar=-1.0, in1=thrB,
                            op0=ALU.mult, op1=ALU.is_ge),
                        lambda e, dst=dst: e.tensor_tensor(
                            out=dst, in0=dib3[:, 0:1024],
                            in1=dib3[:, 1024:2048], op=ALU.subtract))
                mwd = w.v("cv")

                for t in range(NT):
                    half = (t % 2) * 1024
                    if t == 0:
                        w.wait("tensor", "dhtr", mtr)
                        w.wait("tensor", "cv", mwd)
                        if blk == 0:
                            ps_s, ps_m = guard["pS_free"]
                            w.wait("tensor", ps_s, ps_m)
                            w.wait("tensor", "cv", res_ct[-1])
                            ees, eem = guard["ee_free"]
                            w.wait("tensor", ees, eem)
                    if t >= 2:
                        w.wait("tensor", "cv", first_pass_marks[t - 2])
                    mdn = 0
                    for icl in range(8):
                        if icl < 5:
                            lsrc = S["dd"][:, icl * 2048 + t * 128:
                                           icl * 2048 + (t + 1) * 128]
                        elif icl < 7:
                            lsrc = S["ff"][:, (icl - 5) * 2048 + t * 128:
                                           (icl - 5) * 2048 +
                                           (t + 1) * 128]
                        else:
                            lsrc = S["hh"][:, t * 128:(t + 1) * 128]
                        for f2 in range(2):
                            last = (icl == 7 and f2 == 1)
                            mdn = w.do("tensor",
                                       lambda e, icl=icl, f2=f2, half=half,
                                       lsrc=lsrc: e.matmul(
                                           pS[:, half + f2 * 512:
                                              half + (f2 + 1) * 512],
                                           lsrc,
                                           bass.AP(S["ee"],
                                                   icl * 1024 + f2 * 512,
                                                   [[8192, 128], [1, 512]]),
                                           start=(icl == 0),
                                           stop=(icl == 7)),
                                       inc="ct" if last else None) or mdn
                    w.wait("vector", "ct", mdn)
                    at_ = S["x"][:, t * 1024:(t + 1) * 1024]
                    if blk == 0:
                        w.wait("vector", "dpark", mpark)
                        w.do("vector", lambda e, at_=at_, half=half:
                             e.tensor_copy(out=at_,
                                           in_=pS[:, half:half + 1024]),
                             inc="cv")
                    else:
                        w.do("vector", lambda e, at_=at_, half=half:
                             e.tensor_tensor(out=at_, in0=at_,
                                             in1=pS[:, half:half + 1024],
                                             op=ALU.add), inc="cv")
                    first_pass_marks[t] = w.v("cv")
                mk["dn_ct"] = w.v("ct")

            xpv = S["di"].ap().bitcast(F32)
            out_done = {}
            for t in range(NT):
                slot = t % 2
                w.wait("sync", "dpark", mpark)
                w.wait("sync", "cv", out_done.get(t - 2, 0))
                w.wait("sync", "dout", w.v("dout"))
                w.wait("sync", "cv", mwd)
                w.dma("sync", xpv[:, slot * 1024:(slot + 1) * 1024],
                      bass.AP(xp_h, t * 128 * 1024, [[1024, 128], [1, 1024]]),
                      "dxp")
                mxp = w.v("dxp")
                at_ = S["x"][:, t * 1024:(t + 1) * 1024]
                w.wait("vector", "cv", first_pass_marks[t])
                mres = post_ln_tile_sbuf(t, at_, xpv, slot, mxp)
                out_done[t] = mres
                w.wait("sync", "cv", mres)
                w.dma("sync",
                      bass.AP(y_out, t * 128 * 1024, [[1024, 128], [1, 1024]]),
                      xpv[:, slot * 1024:(slot + 1) * 1024], "dout")

        def post_ln_tile_sbuf(t, acc_ap, xpv, slot, mxp):
            sq = S["dw"][:, 1024:2048]
            w.wait("scalar", "cv", w.v("cv"))
            msq = w.do("scalar", lambda e, acc_ap=acc_ap, sq=sq:
                       e.activation(out=sq, in_=acc_ap, func=AF.Square),
                       inc="cs")
            vchain(lambda e, acc_ap=acc_ap: e.reduce_sum(
                out=S["st"][:, 64:65], in_=acc_ap, axis=AX.X))
            w.wait("vector", "cs", msq)
            mv = vchain(
                lambda e, sq=sq: e.reduce_sum(
                    out=S["st"][:, 65:66], in_=sq, axis=AX.X),
                lambda e: e.tensor_scalar(
                    out=S["st"][:, 64:65], in0=S["st"][:, 64:65],
                    scalar1=msl("invD"), scalar2=None, op0=ALU.mult),
                lambda e: e.tensor_scalar(
                    out=S["st"][:, 65:66], in0=S["st"][:, 65:66],
                    scalar1=msl("invD"), scalar2=None, op0=ALU.mult),
                lambda e: e.tensor_tensor(
                    out=S["st"][:, 67:68], in0=S["st"][:, 64:65],
                    in1=S["st"][:, 64:65], op=ALU.mult),
                lambda e: e.tensor_tensor(
                    out=S["st"][:, 65:66], in0=S["st"][:, 65:66],
                    in1=S["st"][:, 67:68], op=ALU.subtract),
                lambda e: e.tensor_scalar(
                    out=S["st"][:, 66:67], in0=S["st"][:, 65:66],
                    scalar1=msl("eps"), scalar2=None, op0=ALU.add))
            w.wait("scalar", "cv", mv)
            _m = w.do("scalar", lambda e: e.activation(
                out=S["st"][:, 67:68], in_=S["st"][:, 66:67], func=AF.Ln),
                inc="cs")
            w.wait("scalar", "cs", _m)
            w.do("scalar", lambda e: e.activation(
                out=S["st"][:, 66:67], in_=S["st"][:, 67:68], func=AF.Exp,
                scale=msl("mhalf")), inc="cs")
            mr = w.v("cs")
            w.wait("vector", "cs", mr)
            mmr3 = w.do("vector", lambda e: e.tensor_tensor(
                out=S["st"][:, 67:68], in0=S["st"][:, 64:65],
                in1=S["st"][:, 66:67], op=ALU.mult), inc="cv")
            w.wait("vector", "cv", mmr3)
            mtsq = w.do("vector", lambda e: e.tensor_scalar(
                out=acc_ap, in0=acc_ap, scalar1=S["st"][:, 66:67],
                scalar2=S["st"][:, 67:68], op0=ALU.mult, op1=ALU.subtract),
                inc="cv")
            w.wait("vector", "cv", mtsq)
            w.wait("vector", "dxp", mxp)
            dst = xpv[:, slot * 1024:(slot + 1) * 1024]
            w.do("vector", lambda e, dst=dst: e.tensor_tensor(
                out=dst, in0=dst, in1=acc_ap, op=ALU.add), inc="cv")
            return w.v("cv")

        def phase_out(mark):
            w.wait("sync", "cv", mark)
            for t in range(NT):
                w.dma("sync",
                      bass.AP(y_out, t * 128 * 1024, [[1024, 128], [1, 1024]]),
                      S["x"][:, t * 1024:(t + 1) * 1024], "dout")
            w.wait("sync", "dout", w.v("dout"))

        if PHASE == "p0":
            phase_out(mk["mflip"])
            return
        lnmk0 = ln_pre(mk["consts"])
        if DEBUG:
            w.wait("sync", "dxho", lnmk0["xho"] if "xho" in lnmk0 else 0)
            for t in range(NT):
                w.dma("sync", bass.AP(dbg3, t * 128 * 1024,
                                      [[1024, 128], [1, 1024]]),
                      S["bg"][:, t * 1024:(t + 1) * 1024], "dout")
        if PHASE == "ln0":
            w.wait("sync", "dxtr", lnmk0["xtr"])
            if DEBUG:
                for t in range(NT):
                    w.dma("sync", bass.AP(dbg3, t * 128 * 1024,
                                          [[1024, 128], [1, 1024]]),
                          S["bg"][:, t * 1024:(t + 1) * 1024], "dout")
            phase_out(0)
            return
        r0 = attn_layer(0, lnmk0)
        if PHASE in ("pre0", "qk0", "qkv0", "core0", "core0z",
                     "otr0", "obb0", "attn0") or                 PHASE.startswith(("core0d", "opjb", "opjt", "pln")):
            if PHASE == "attn0" and DEBUG:
                w.wait("sync", "cv", r0)
                for t in range(NT):
                    w.dma("sync", bass.AP(dbg2, t * 128 * 1024,
                                          [[1024, 128], [1, 1024]]),
                          S["bg"][:, t * 1024:(t + 1) * 1024], "dout")
            phase_out(r0)
            return
        if PHASE == "attn0":
            phase_out(r0)
            return
        if DEBUG:
            w.wait("sync", "cv", r0)
            for t in range(NT):
                w.dma("sync", bass.AP(dbg0, t * 128 * 1024,
                                      [[1024, 128], [1, 1024]]),
                      S["x"][:, t * 1024:(t + 1) * 1024], "dout")
        r1 = l0ffn(r0)
        if PHASE == "ffn0":
            phase_out(r1)
            return
        if DEBUG:
            w.wait("sync", "cv", r1)
            for t in range(NT):
                w.dma("sync", bass.AP(dbg1, t * 128 * 1024,
                                      [[1024, 128], [1, 1024]]),
                      S["x"][:, t * 1024:(t + 1) * 1024], "dout")
        lnmk1 = ln_pre(r1)
        r2 = attn_layer(1, lnmk1)
        if PHASE == "attn1":
            phase_out(r2)
            return
        l1ffn(r2)

        w.wait("sync", "dout", w.v("dout"))

    def mkwalk(name):
        def f(eng):
            program(Wk(name, eng, sems))
        return f

    block.sync(mkwalk("sync"))
    block.gpsimd(mkwalk("gpsimd"))
    block.tensor(mkwalk("tensor"))
    block.vector(mkwalk("vector"))
    block.scalar(mkwalk("scalar"))

    blk_cm.__exit__(None, None, None)
    for cm in reversed(sem_cm):
        cm.__exit__(None, None, None)
    return nc


_NC_CACHE = {}


def _host_inputs(inputs):
    f = np.ascontiguousarray

    def f32(a):
        return f(np.asarray(a, dtype=np.float32))

    relp = f32(inputs["L0_attn_rel_pos"])
    relpT = np.zeros((64, 4096), np.float32)
    relpT[:, 0:4095] = relp.T

    hd = HD = 64
    inv = 1.0 / (10000.0 ** (np.arange(0, hd, 2, dtype=np.float64) / hd))
    n = np.arange(T, dtype=np.float64)
    fr = np.repeat(np.outer(n, inv), 2, axis=1)
    cosT = np.zeros((128, T), np.float32)
    sinT = np.zeros((128, T), np.float32)
    cosT[0:64, :] = np.cos(fr).T.astype(np.float32)
    cosT[64:128, :] = cosT[0:64, :]
    sinT[0:64, :] = np.sin(fr).T.astype(np.float32)
    sinT[64:128, :] = sinT[0:64, :]

    tri = (np.arange(128)[None, :] >= np.arange(128)[:, None]) \
        .astype(np.float32)
    P = np.zeros((64, 64), np.float32)
    for i in range(32):
        P[2 * i, 2 * i + 1] = -1.0
        P[2 * i + 1, 2 * i] = 1.0
    PT = np.zeros((128, 128), np.float32)
    PT[0:64, 0:64] = P.T
    PT[64:128, 64:128] = P.T

    com = dict(
        qkvw0=f32(inputs["L0_attn_qkv_w"]).T.copy(),
        qkvb0=f32(inputs["L0_attn_qkv_b"]).reshape(24, 128).T.copy(),
        qvb0=f32(inputs["L0_attn_qkv_b"])[2048:].reshape(1, D),
        ow0=f32(inputs["L0_attn_out_w"]).T.copy(),
        ob0=f32(inputs["L0_attn_out_b"]).reshape(1, D),
        relpT=relpT,
        rpew=f32(inputs["L0_attn_rpe_w"]).reshape(64, 1),
        Wexp=f32(inputs["L0_ffn_W"]),
        bexp=f32(inputs["L0_ffn_b"]),
        logit8=f32(inputs["L0_ffn_logits"]).reshape(1, 8),
        u8=f32(inputs["L0_ffn_u"]).reshape(1, 8),
        qkvw1=f32(inputs["L1_attn_Wqkv"]),
        ow1=f32(inputs["L1_attn_Wout"]),
        ob1=f32(inputs["L1_attn_bout"]).reshape(1, D),
        wgT=f32(inputs["L1_ffn_w_gate"]).T.copy(),
        wuT=f32(inputs["L1_ffn_w_up"]).T.copy(),
        wdT=f32(inputs["L1_ffn_w_down"]).T.copy(),
        cosTc=cosT, sinTc=sinT, tri_c=tri, p128c=PT,
        iota8=np.arange(8, dtype=np.float32).reshape(1, 8),
    )
    return com


def kernel(**inputs):
    if "nc" not in _NC_CACHE:
        _NC_CACHE["nc"] = build_nc()
    nc = _NC_CACHE["nc"]
    com = _host_inputs(inputs)
    x = np.ascontiguousarray(np.asarray(inputs["L0_attn_x"],
                                        dtype=np.float32))
    in_maps = []
    for b in range(8):
        m = dict(com)
        m["x_in"] = x[b]
        in_maps.append(m)
    res = run_bass_kernel_spmd(nc, in_maps, core_ids=list(range(8)))
    out = np.stack([res.results[b]["y_out"] for b in range(8)], axis=0)
    return out.astype(np.float32)
